# version 33
# baseline (speedup 1.0000x reference)
import math
import numpy as np
import ml_dtypes
import concourse.bass as bass
import concourse.mybir as mybir
from concourse.bass_utils import run_bass_kernel_spmd

F32 = mybir.dt.float32
BF16 = mybir.dt.bfloat16
I32 = mybir.dt.int32
U8 = mybir.dt.uint8
AF = mybir.ActivationFunctionType
ALU = mybir.AluOpType

PI = math.pi
CMAG = 1.5 * 2 ** 23
NCORES = 8
T = 1024
NCH = 16
NQH, NKH = 16, 8
NHC = 44
NQUART = 4
RING = 4

ARENA_BYTES = 200704
O_CONST = 0
O_RING = 8192
O_A = 24576
O_B = 57344
O_C = 122880
O_D = 155648
O_E = 188416

OC_COS = 0
OC_SIN = 2048
OC_BIAS = 4096
OC_CBF = 4736
OC_ONESR_BF = 5824
OC_ONESR_F32 = 6080
OC_RTS = 6592
OC_BV64 = 6848
BC_BQ, BC_BK, BC_BV, BC_BO = 0, 16, 24, 32
BC_SG, BC_SU, BC_SD = 48, 92, 136
BC_CMAG, BC_B025, BC_B2, BC_EPS, BC_EPS12 = 152, 153, 154, 155, 156
CB_ID, CB_MA, CB_MB, CB_ONEC, CB_OAVG = 0, 128, 256, 384, 392

_BUILD_CACHE = {}


def _build():
    if "nc" in _BUILD_CACHE:
        return _BUILD_CACHE["nc"]
    nc = bass.Bass("TRN2", target_bir_lowering=False, debug=False)

    d_xc = nc.dram_tensor("xc", [128, NCH * T], F32, kind="ExternalInput").ap()
    d_xs = nc.dram_tensor("xs", [128, NCH * T], F32, kind="ExternalInput").ap()
    d_cos = nc.dram_tensor("cosd", [128, 512], BF16, kind="ExternalInput").ap()
    d_sin = nc.dram_tensor("sind", [128, 512], BF16, kind="ExternalInput").ap()
    d_bias = nc.dram_tensor("biasblob", [128, 160], F32, kind="ExternalInput").ap()
    d_cbf = nc.dram_tensor("cbf", [128, 520], BF16, kind="ExternalInput").ap()
    d_or_bf = nc.dram_tensor("ones_r_bf", [1, 128], BF16, kind="ExternalInput").ap()
    d_or_f = nc.dram_tensor("ones_r_f32", [1, 128], F32, kind="ExternalInput").ap()
    d_rts = nc.dram_tensor("rts", [128, 128], F32, kind="ExternalInput").ap()
    d_wq = nc.dram_tensor("wqs", [NQH, 128, 2048], F32, kind="ExternalInput").ap()
    d_wk = nc.dram_tensor("wks", [NKH, 128, 2048], F32, kind="ExternalInput").ap()
    d_wv = nc.dram_tensor("wvs", [NKH, 128, 2048], F32, kind="ExternalInput").ap()
    d_wo = nc.dram_tensor("wos", [NCH, 128, 2048], F32, kind="ExternalInput").ap()
    d_wg = nc.dram_tensor("wgs", [NHC, 128, 2048], U8, kind="ExternalInput").ap()
    d_wu = nc.dram_tensor("wus", [NHC, 128, 2048], U8, kind="ExternalInput").ap()
    d_wd = nc.dram_tensor("wds", [NHC, 128, 2048], U8, kind="ExternalInput").ap()
    d_y = nc.dram_tensor("yT", [128, NCH * T], F32, kind="ExternalOutput").ap()

    from contextlib import ExitStack
    ctx = ExitStack()
    arena_t = ctx.enter_context(nc.sbuf_tensor("arena", [128, ARENA_BYTES], U8))
    parena_t = ctx.enter_context(nc.psum_tensor("parena", [128, 4096], F32))
    sWp = [ctx.enter_context(nc.semaphore(f"sW{i}")) for i in range(8)]
    sXp = [ctx.enter_context(nc.semaphore(f"sX{i}")) for i in range(8)]
    sPE = ctx.enter_context(nc.semaphore("sPE"))
    sV = ctx.enter_context(nc.semaphore("sV"))
    sS = ctx.enter_context(nc.semaphore("sS"))
    sO = ctx.enter_context(nc.semaphore("sO"))
    sB = ctx.enter_context(nc.semaphore("sB"))
    sG = ctx.enter_context(nc.semaphore("sG"))
    block = ctx.enter_context(nc.Block())

    AR = arena_t[:]
    PA = parena_t[:]

    def cv(off, dt, parts, elems):
        b = AR.bitcast(dt)
        es = {F32: 4, BF16: 2, I32: 4, U8: 1}[dt]
        assert off % es == 0
        return b[0:parts, off // es: off // es + elems]

    def pv(off_elems, parts, elems):
        return PA[0:parts, off_elems: off_elems + elems]

    def warm(n, off=2048):
        dst = PA[0:128, off:off + 512]
        for _ in range(n):
            TE.matmul(dst, v_ident, v_cbf[:, 0:512], start=True, stop=True)

    v_cos = cv(O_CONST + OC_COS, BF16, 128, 512)
    v_sin = cv(O_CONST + OC_SIN, BF16, 128, 512)
    v_bias = cv(O_CONST + OC_BIAS, F32, 128, 160)
    v_cbf = cv(O_CONST + OC_CBF, BF16, 128, 520)
    v_or_bf = cv(O_CONST + OC_ONESR_BF, BF16, 1, 128)
    v_or_f = cv(O_CONST + OC_ONESR_F32, F32, 1, 128)
    v_rts = cv(O_CONST + OC_RTS, BF16, 128, 128)
    v_bv64 = cv(O_CONST + OC_BV64, F32, 128, 8)
    v_ident = v_cbf[:, CB_ID:CB_ID + 128]
    v_maskA = v_cbf[:, CB_MA:CB_MA + 128]
    v_maskB = v_cbf[:, CB_MB:CB_MB + 128]
    v_onec = v_cbf[:, CB_ONEC:CB_ONEC + 1]
    v_oavg = v_cbf[:, CB_OAVG:CB_OAVG + 128]

    def bcol(idx):
        return v_bias[:, idx:idx + 1]

    ring = [cv(O_RING + i * 4096, BF16, 128, 2048) for i in range(RING)]
    aring = [cv(O_D + i * 4096, BF16, 128, 2048) for i in range(8)]
    wd_tiles = [cv((O_D + 4096 * i) if i < 8 else (O_E + 4096 * (i - 8)), BF16, 128, 2048)
                for i in range(11)]

    v_xaT = cv(O_A, BF16, 128, NCH * T)
    v_xsr = [cv(O_A + i * 4096, F32, 128, T) for i in range(2)]
    v_xgT = cv(O_A, BF16, 128, NCH * T)
    v_xc = [cv(O_B + ch * 4096, F32, 128, T) for ch in range(NCH)]
    v_qpT = cv(O_B, BF16, 128, NQH * T)
    v_kpT = cv(O_B + 32768, BF16, 128, NKH * T)
    v_vT = cv(O_B + 49152, BF16, 128, NKH * T)
    v_h1 = cv(O_B, F32, 128, NCH * T)
    v_scr = cv(O_D, BF16, 128, NCH * T)

    v_vsc = cv(O_E, F32, 128, T)
    v_rt2 = cv(O_E + 4096, BF16, 128, T)
    v_rt4 = cv(O_E + 6144, BF16, 128, T)
    v_xbf = [cv(O_C + i * 2048, BF16, 128, T) for i in range(2)]
    v_sqx = [cv(O_C + 4096 + i * 2048, BF16, 128, T) for i in range(2)]
    v_mb = cv(O_C + 8192, F32, 128, T)
    v_ib = cv(O_C + 12288, F32, 128, T)
    v_row = [cv(O_C + 16384 + i * 4096, F32, 1, T) for i in range(3)]
    v_t1 = cv(O_C + 28672, F32, 128, T)
    v_qel2 = [cv(O_C + i * 2048, BF16, 128, T) for i in range(2)]
    v_qeh2 = [cv(O_C + 4096 + i * 2048, BF16, 128, T) for i in range(2)]
    v_qpl2 = [cv(O_C + 8192 + i * 2048, BF16, 128, T) for i in range(2)]
    v_qph2 = [cv(O_C + 12288 + i * 2048, BF16, 128, T) for i in range(2)]
    v_sql = cv(O_C + 16384, BF16, 128, T)
    v_sqh = cv(O_C + 18432, BF16, 128, T)
    v_rn2 = [cv(O_C + 20480 + i * 2048, BF16, 128, T) for i in range(2)]
    v_r0 = cv(O_C + 24576, F32, 128, T)
    v_r2 = cv(O_C + 28672, F32, 128, T)
    v_kpp = cv(O_C, BF16, 128, 4096)
    v_vjp = [cv(O_C + 8192 + i * 2048, BF16, 64, T) for i in range(2)]
    v_a2 = [cv(O_C + 12288 + i * 2048, BF16, 64, T) for i in range(2)]
    v_gu = [cv(O_C + hc * 2048, BF16, 128, T) for hc in range(11)]
    v_silu2 = [cv(O_C + 22528 + i * 2048, BF16, 128, T) for i in range(2)]
    v_upt2 = [cv(O_C + 26624 + i * 2048, BF16, 128, T) for i in range(2)]
    v_corr = cv(O_C + 30720, BF16, 1, T)

    TE, VE, SE, SY, GP = nc.tensor, nc.vector, nc.scalar, nc.sync, nc.gpsimd

    cnt = dict(PE=0, V=0, S=0, O=0, B=0, G=0)
    SEM = dict(PE=sPE, V=sV, S=sS, O=sO, B=sB, G=sG)

    def wxl(eng, i):
        eng.wait_ge(sXp[i % 8], 16 * (i // 8 + 1))

    def wwl(eng, i):
        eng.wait_ge(sWp[i % 8], 16 * (i // 8 + 1))

    def inc(instr, s, n=1):
        cnt[s] += n
        instr.then_inc(SEM[s], n)
        return cnt[s]

    def wge(eng, s, val):
        if val > 0:
            eng.wait_ge(SEM[s], val)

    xload = []
    xload.append((v_cos, d_cos))
    xload.append((v_sin, d_sin))
    xload.append((v_bias, d_bias))
    xload.append((v_cbf, d_cbf))
    xload.append((v_or_bf, d_or_bf))
    xload.append((v_or_f, d_or_f))
    IX_CONST_LAST = 5
    IX_XC0 = 6
    for ch in range(NCH):
        xload.append((v_xc[ch], d_xc[:, ch * T:(ch + 1) * T]))

    wload = []
    wload.append(dict(dst=v_rts, src=d_rts, wait=None))
    W_RT = 0
    ring_blocks = []
    ring_consumed_mark = {}
    aring_consumed_mark = {}

    aring_blocks = []
    ndma = [1]

    def add_ring_block(src, attn=False):
        gi = ndma[0]
        ndma[0] += 1
        if attn:
            rseq = len(aring_blocks)
            wload.append(dict(dst=aring[rseq % 8], src=src, wait=("aring", rseq)))
            aring_blocks.append(gi)
            return gi, rseq
        rseq = len(ring_blocks)
        wload.append(dict(dst=ring[rseq % RING], src=src, wait=("ring", rseq)))
        ring_blocks.append(gi)
        return gi, rseq

    bar_val = [0]

    def barrier():
        for eng in (TE, VE, SE):
            eng.drain()
            eng.sem_inc(sB, 1)
        bar_val[0] += 3
        for eng in (TE, VE, SE):
            wge(eng, "B", bar_val[0])
        cnt["B"] = bar_val[0]

    for k, (dst, src) in enumerate(xload):
        eng = SE if (IX_XC0 + 8 <= k < IX_XC0 + 16) else SY
        if k >= 8:
            eng.wait_ge(sXp[k % 8], 16 * (k // 8))
        i = eng.dma_start(out=dst, in_=src)
        i.then_inc(sXp[k % 8], 16)

    for i in range(IX_CONST_LAST + 1):
        wxl(VE, i)
    VE.tensor_scalar(out=v_bv64, in0=v_bias[:, BC_BV:BC_BV + 8], scalar1=1.0 / 64.0,
                     scalar2=None, op0=ALU.mult)
    m_bv64 = inc(nc.vector.engine_nop(), "V")

    p_mb = pv(0, 128, T)
    p_sqb = pv(1024, 128, T)
    m_statmm = {}
    for _i in range(IX_CONST_LAST + 1):
        wxl(TE, _i)
    for ch in range(NCH):
        buf = ch % 2
        wxl(VE, IX_XC0 + ch)
        if ch >= 2:
            wge(VE, "PE", m_statmm[ch - 2])
        VE.tensor_copy(out=v_xbf[buf], in_=v_xc[ch])
        i = VE.tensor_mul(out=v_sqx[buf], in0=v_xc[ch], in1=v_xc[ch])
        m_cast = inc(i, "V")
        warm(4)
        wge(TE, "V", m_cast)
        for th in range(2):
            TE.matmul(p_mb[:, th * 512:(th + 1) * 512], v_oavg,
                      v_xbf[buf][:, th * 512:(th + 1) * 512], start=(ch == 0), stop=(ch == NCH - 1))
        for th in range(2):
            i = TE.matmul(p_sqb[:, th * 512:(th + 1) * 512], v_oavg,
                          v_sqx[buf][:, th * 512:(th + 1) * 512], start=(ch == 0), stop=(ch == NCH - 1))
        m_statmm[ch] = inc(i, "PE")
    warm(40)
    wge(VE, "PE", m_statmm[NCH - 1])
    VE.tensor_copy(out=v_mb, in_=p_mb)
    VE.tensor_mul(out=v_ib, in0=v_mb, in1=v_mb)
    i = VE.tensor_sub(out=v_ib, in0=p_sqb, in1=v_ib)
    m_var = inc(i, "V")
    wge(SE, "V", m_var)
    for _i in range(IX_CONST_LAST + 1):
        wxl(SE, _i)
    SE.activation(out=v_ib, in_=v_ib, func=AF.Ln, bias=bcol(BC_EPS))
    i = SE.activation(out=v_ib, in_=v_ib, func=AF.Exp, scale=-0.5)
    m_bcsb = inc(i, "S")
    m_xa_chunk = {}
    wge(VE, "S", m_bcsb)
    for th in range(2):
        sl = slice(th * 512, (th + 1) * 512)
        for ch in range(NCH):
            VE.tensor_sub(out=v_t1[:, sl], in0=v_xc[ch][:, sl], in1=v_mb[:, sl])
            i = VE.tensor_mul(out=v_xaT[:, ch * T + th * 512: ch * T + th * 512 + 512],
                              in0=v_t1[:, sl], in1=v_ib[:, sl])
            m_xa_chunk[(ch, th)] = inc(i, "V")
    m_xa_done = cnt["V"]

    p_lo = pv(0, 128, T)
    p_hi = pv(1024, 128, T)
    p_p2 = pv(2048, 128, T)
    p_rnr = pv(3072, 128, T)
    m_r0_prev = 0
    m_clamp_prev = 0
    m_evac_prev = 0
    m_rope_prev = 0
    m_p2_prev = 0
    m_feat_prev = 0
    m_feat = {}
    wwl(TE, W_RT)

    pairs = [("k", p) for p in range(NKH // 2)] + [("q", p) for p in range(NQH // 2)]
    proj_mark = {}
    _evac_prev = [0]

    def emit_proj(pi):
        fam, P = pairs[pi]
        dsrc = d_wq if fam == "q" else d_wk
        (gl, rl) = add_ring_block(dsrc[2 * P + 0], attn=True)
        (gh, rh) = add_ring_block(dsrc[2 * P + 1], attn=True)
        for half, (gidx, rseq, psum) in enumerate(((gl, rl, p_lo), (gh, rh, p_hi))):
            wwl(TE, gidx)
            if _evac_prev[0]:
                wge(TE, "S", _evac_prev[0])
            slot = aring[rseq % 8]
            for th in range(2):
                for ch in range(NCH):
                    i = TE.matmul(psum[:, th * 512:(th + 1) * 512],
                                  slot[:, ch * 128:(ch + 1) * 128],
                                  v_xaT[:, ch * T + th * 512: ch * T + th * 512 + 512],
                                  start=(ch == 0), stop=(ch == NCH - 1))
            aring_consumed_mark[rseq] = inc(i, "PE")
        proj_mark[pi] = cnt["PE"]


    m_rope_hist = {}
    m_p2_hist = {}
    m_mask_hist = {}
    m_evac_hist = {}

    def emit_evac(pi):
        fam, P = pairs[pi]
        par = pi % 2
        bc0 = BC_BQ if fam == "q" else BC_BK
        wge(SE, "PE", proj_mark[pi])
        if pi >= 2:
            wge(SE, "V", m_rope_hist[pi - 2])
        SE.add(out=v_qel2[par], in_=p_lo, add=bcol(bc0 + 2 * P + 0))
        i = SE.add(out=v_qeh2[par], in_=p_hi, add=bcol(bc0 + 2 * P + 1))
        m_evac_hist[pi] = inc(i, "S")
        _evac_prev[0] = m_evac_hist[pi]

    m_vevac = {}

    def emit_vplane(j):
        (gi, rseq) = add_ring_block(d_wv[j], attn=True)
        wwl(TE, gi)
        if m_vevac.get(j - 2):
            wge(TE, "S", m_vevac[j - 2])
        psum = p_lo if j % 2 == 0 else p_hi
        slot = aring[rseq % 8]
        for th in range(2):
            for ch in range(NCH):
                if j == 0:
                    warm(1, off=3072)
                    wge(TE, "V", m_xa_chunk[(ch, th)])
                i = TE.matmul(psum[:, th * 512:(th + 1) * 512],
                              slot[:, ch * 128:(ch + 1) * 128],
                              v_xaT[:, ch * T + th * 512: ch * T + th * 512 + 512],
                              start=(ch == 0), stop=(ch == NCH - 1))
        aring_consumed_mark[rseq] = inc(i, "PE")
        m_vmm = cnt["PE"]
        wge(SE, "PE", m_vmm)
        wge(SE, "V", m_bv64)
        SE.mul(out=v_vsc, in_=psum, mul=1.0 / 64.0)
        vdst = v_vT.rearrange("p (t j) -> p j t", j=NKH)[:, j:j + 1, :]
        i = SE.add(out=vdst, in_=v_vsc, add=v_bv64[:, j:j + 1])
        m_vevac[j] = inc(i, "S")
        _evac_prev[0] = m_vevac[j]

    for _j in range(NKH):
        emit_vplane(_j)
    emit_proj(0)
    emit_evac(0)
    for pi, (fam, P) in enumerate(pairs):
        par = pi % 2
        v_qel = v_qel2[par]
        v_qeh = v_qeh2[par]
        v_qpl = v_qpl2[par]
        v_qph = v_qph2[par]
        dstT = v_qpT if fam == "q" else v_kpT
        bc0 = BC_BQ if fam == "q" else BC_BK
        m_evac_prev = m_evac_hist[pi]
        if pi + 1 < len(pairs):
            emit_proj(pi + 1)
        wge(VE, "S", m_evac_prev)
        if pi >= 1:
            wge(VE, "PE", m_mask_hist[pi - 1])
        VE.tensor_mul(out=v_sql, in0=v_qel, in1=v_qel)
        i = VE.tensor_mul(out=v_sqh, in0=v_qeh, in1=v_qeh)
        m_sq = inc(i, "V")
        if pi >= 2:
            wge(VE, "PE", m_p2_hist[pi - 2])
        for bh in range(2):
            sl = slice(bh * 512, (bh + 1) * 512)
            VE.tensor_mul(out=v_rt2[:, sl], in0=v_qeh[:, sl], in1=v_sin)
            VE.tensor_mul(out=v_rt4[:, sl], in0=v_qel[:, sl], in1=v_sin)
            VE.tensor_mul(out=v_qel[:, sl], in0=v_qel[:, sl], in1=v_cos)
            VE.tensor_mul(out=v_qeh[:, sl], in0=v_qeh[:, sl], in1=v_cos)
            VE.tensor_sub(out=v_qpl[:, sl], in0=v_qel[:, sl], in1=v_rt2[:, sl])
            i = VE.tensor_add(out=v_qph[:, sl], in0=v_qeh[:, sl], in1=v_rt4[:, sl])
        m_rope = inc(i, "V")
        m_rope_hist[pi] = m_rope
        m_rn_h = {}
        for hh in range(2):
            mask = v_maskA if hh == 0 else v_maskB
            rn = v_rn2[hh]
            wge(TE, "V", m_sq)
            if m_clamp_prev:
                wge(TE, "S", m_clamp_prev)
            for th in range(2):
                sl = slice(th * 512, (th + 1) * 512)
                TE.matmul(p_rnr[:, sl], mask, v_sql[:, sl], start=True, stop=False)
                i = TE.matmul(p_rnr[:, sl], mask, v_sqh[:, sl], start=False, stop=True)
            m_maskmm_last = inc(i, "PE")
            m_mask_hist[pi] = m_maskmm_last
            wge(SE, "PE", m_maskmm_last)
            SE.activation(out=p_rnr, in_=p_rnr, func=AF.Ln, bias=bcol(BC_EPS12))
            i = SE.activation(out=rn, in_=p_rnr, func=AF.Exp, scale=-0.5)
            m_rn_h[hh] = inc(i, "S")
            m_clamp_prev = m_rn_h[hh]
        if pi + 1 < len(pairs):
            emit_evac(pi + 1)
        for hh in range(2):
            head = 2 * P + hh
            beta = 64 * hh
            rn = v_rn2[hh]
            wge(TE, "V", m_rope)
            if m_r0_prev:
                wge(TE, "V", m_r0_prev)
            for dbeta in (0, 64):
                for th in range(2):
                    sl = slice(th * 512, (th + 1) * 512)
                    TE.matmul(p_p2[dbeta:dbeta + 64, sl], v_rts[beta:beta + 64, 0:64],
                              v_qpl[beta:beta + 64, sl], start=True, stop=False)
                    i = TE.matmul(p_p2[dbeta:dbeta + 64, sl], v_rts[beta:beta + 64, 64:128],
                                  v_qph[beta:beta + 64, sl], start=False, stop=True)
            m_p2 = inc(i, "PE")
            m_p2_hist[pi] = m_p2
            wge(VE, "PE", m_p2)
            wge(VE, "S", m_rn_h[hh])
            if m_feat_prev:
                wge(VE, "S", m_feat_prev)
            i = VE.tensor_mul(out=v_r0, in0=p_p2, in1=rn)
            m_r0_prev = inc(i, "V")
            VE.tensor_scalar(out=v_r2, in0=v_r0, scalar1=1.0 / (2 * PI),
                             scalar2=bcol(BC_B025), op0=ALU.mult, op1=ALU.add)
            VE.tensor_scalar(out=v_r2, in0=v_r2, scalar1=CMAG, scalar2=CMAG,
                             op0=ALU.add, op1=ALU.subtract)
            i = VE.scalar_tensor_tensor(out=v_r2, in0=v_r2, scalar=-2 * PI,
                                        in1=v_r0, op0=ALU.mult, op1=ALU.add)
            m_w = inc(i, "V")
            wge(SE, "V", m_w)
            i = SE.activation(out=dstT[:, head * T:(head + 1) * T], in_=v_r2,
                              func=AF.Sin, bias=bcol(BC_B2))
            m_feat[(fam, head)] = inc(i, "S")
            m_feat_prev = m_feat[(fam, head)]
            m_p2_prev = m_p2
        m_rope_prev = m_rope

    p_a2 = [pv(0, 64, T), pv(1024, 64, T)]
    p_out2 = [pv(2048, 128, T), pv(3072, 128, T)]
    p_vjp2 = [PA[64:128, 0:1024], PA[64:128, 1024:2048]]
    m_kpp = {}
    m_a2ev = {}
    m_vjpev = {}
    m_mm1 = {}
    m_mm2 = {}
    m_vjpmm = {}
    m_screvac = {}

    def emit_kppad(mt):
        t0k = mt * 64
        wge(SE, "V", m_ms)
        if mt >= 1:
            wge(SE, "PE", m_mm1[mt - 1])
        for r4 in range(4):
            for par in range(2):
                tau0 = r4 + 4 * par
                src = v_kpT.rearrange("p (j t) -> p j t", j=NKH)[:, :, t0k + tau0::8][:, :, 0:8]
                src = src.rearrange("p j t -> p t j")
                dst = v_kpp.rearrange("p (t s) -> p t s", s=64)[:, tau0::8,
                                                               32 * par + r4 * 8: 32 * par + r4 * 8 + 8]
                i = SE.copy(out=dst, in_=src)
        m_kpp[mt] = inc(i, "S")

    def emit_vjpmm(m):
        if m >= 2:
            wge(TE, "V", m_vjpev[m - 2])
        dst = p_vjp2[m % 2]
        t0m = m * 64
        for g8 in range(8):
            gpos = t0m + g8 * 8
            lhsT = v_vT[:, gpos * 8: gpos * 8 + 64]
            i = TE.matmul(dst[:, g8 * 128:(g8 + 1) * 128], lhsT, v_ident,
                          start=True, stop=True)
        m_vjpmm[m] = inc(i, "PE")

    i = VE.memset(v_kpp, 0.0)
    m_ms = inc(i, "V")
    emit_kppad(0)
    wge(TE, "S", m_evac_hist[len(pairs) - 1])
    emit_vjpmm(0)

    barrier()

    for mt in range(16):
        ab = mt % 2
        t0 = mt * 64
        if mt + 1 < 16:
            emit_vjpmm(mt + 1)
        wge(VE, "PE", m_vjpmm[mt])
        if mt >= 2:
            wge(VE, "PE", m_mm2[mt - 2])
        i = VE.tensor_copy(out=v_vjp[ab], in_=p_vjp2[mt % 2])
        m_vjpev[mt] = inc(i, "V")
        wge(TE, "S", m_kpp[mt])
        if mt >= 2:
            wge(TE, "V", m_a2ev[mt - 2])
        qpTh = v_qpT.rearrange("p (h t) -> p t h", h=NQH)
        m_mm1h = {}
        for half in range(2):
            for tau in range(half * 32, half * 32 + 32):
                qslice = qpTh[:, t0 + tau:t0 + tau + 1, :].rearrange("p t h -> p (t h)")
                i = TE.matmul(p_a2[ab][:, tau * 16:(tau + 1) * 16],
                              v_kpp[:, tau * 64:(tau + 1) * 64], qslice,
                              start=True, stop=True)
            m_mm1h[half] = inc(i, "PE")
        m_mm1[mt] = m_mm1h[1]
        m_a2evh = {}
        for half in range(2):
            wge(VE, "PE", m_mm1h[half])
            if mt >= 2:
                wge(VE, "PE", m_mm2[mt - 2])
            sl = slice(half * 512, half * 512 + 512)
            i = VE.tensor_copy(out=v_a2[ab][:, sl], in_=p_a2[ab][:, sl])
            m_a2evh[half] = inc(i, "V")
        m_a2ev[mt] = m_a2evh[1]
        wge(TE, "V", m_vjpev[mt])
        if mt >= 2:
            wge(TE, "V", m_screvac[mt - 2])
        for g8 in range(8):
            wge(TE, "V", m_a2evh[g8 // 4])
            i = TE.matmul(p_out2[mt % 2][:, g8 * 128:(g8 + 1) * 128],
                          v_vjp[ab][:, g8 * 128:(g8 + 1) * 128],
                          v_a2[ab][:, g8 * 128:(g8 + 1) * 128],
                          start=True, stop=True)
        m_mm2[mt] = inc(i, "PE")
        if mt + 1 < 16:
            emit_kppad(mt + 1)
        wge(VE, "PE", m_mm2[mt])
        b = mt // 8
        src = p_out2[mt % 2].rearrange("p (r j h) -> p j r h", r=4, j=16, h=16)
        dst = v_scr.rearrange("p (j b h r) -> p b j r h", j=16, b=2, h=16, r=32)
        dst = dst[:, b:b + 1, :, (mt % 8) * 4:(mt % 8) * 4 + 4, :]
        dst = dst.rearrange("p b j r h -> p (b j) r h")
        i = VE.tensor_copy(out=dst, in_=src)
        m_screvac[mt] = inc(i, "V")

    barrier()

    p_wo = [pv(0, 128, T), pv(1024, 128, T)]
    p_m2 = pv(2048, 128, T)
    p_sq2 = pv(3072, 128, T)
    m_woe = {}
    m_woe_th = {}
    m_womm = {}
    m_statmm2 = {}
    m_cast2 = {}

    def emit_stat2(cc):
        buf = cc % 2
        wge(TE, "V", m_cast2[cc])
        for th in range(2):
            TE.matmul(p_m2[:, th * 512:(th + 1) * 512], v_oavg,
                      v_xbf[buf][:, th * 512:(th + 1) * 512], start=(cc == 0), stop=(cc == NCH - 1))
        for th in range(2):
            i = TE.matmul(p_sq2[:, th * 512:(th + 1) * 512], v_oavg,
                          v_sqx[buf][:, th * 512:(th + 1) * 512], start=(cc == 0), stop=(cc == NCH - 1))
        m_statmm2[cc] = inc(i, "PE")
    for cc in range(NCH):
        (gi, rseq) = add_ring_block(d_wo[cc])
        wwl(TE, gi)
        slot = ring[rseq % RING]
        for th in range(2):
            sl = slice(th * 512, (th + 1) * 512)
            if m_woe_th.get((cc % 2, th)):
                wge(TE, "V", m_woe_th[(cc % 2, th)])
            for ch in range(NCH):
                i = TE.matmul(p_wo[cc % 2][:, sl],
                              slot[:, ch * 128:(ch + 1) * 128],
                              v_scr[:, ch * T + th * 512: ch * T + th * 512 + 512],
                              start=(ch == 0), stop=(ch == NCH - 1))
            m_womm[(cc, th)] = inc(i, "PE")
            wge(VE, "PE", m_womm[(cc, th)])
            wxl(VE, IX_XC0 + NCH + cc)
            i = VE.scalar_tensor_tensor(out=v_h1[:, cc * T + th * 512: cc * T + th * 512 + 512],
                                        in0=p_wo[cc % 2][:, sl],
                                        scalar=bcol(BC_BO + cc), in1=v_xsr[cc % 2][:, sl],
                                        op0=ALU.add, op1=ALU.add)
            m_woe_th[(cc % 2, th)] = inc(i, "V")
        ring_consumed_mark[rseq] = m_womm[(cc, 1)]
        m_woe[cc] = m_woe_th[(cc % 2, 1)]
        buf = cc % 2
        if cc >= 2:
            wge(VE, "PE", m_statmm2[cc - 2])
        h1c = v_h1[:, cc * T:(cc + 1) * T]
        VE.tensor_copy(out=v_xbf[buf], in_=h1c)
        i = VE.tensor_mul(out=v_sqx[buf], in0=h1c, in1=h1c)
        m_cast2[cc] = inc(i, "V")
        if cc >= 1:
            emit_stat2(cc - 1)
        if cc == NCH - 1:
            emit_stat2(cc)

    barrier()

    warm(40, off=0)
    wge(VE, "PE", m_statmm2[NCH - 1])
    VE.tensor_copy(out=v_mb, in_=p_m2)
    VE.tensor_mul(out=v_ib, in0=v_mb, in1=v_mb)
    i = VE.tensor_sub(out=v_ib, in0=p_sq2, in1=v_ib)
    m_var = inc(i, "V")
    wge(SE, "V", m_var)
    SE.activation(out=v_ib, in_=v_ib, func=AF.Ln, bias=bcol(BC_EPS))
    i = SE.activation(out=v_ib, in_=v_ib, func=AF.Exp, scale=-0.5)
    m_bcsb = inc(i, "S")
    m_xg_chunk = {}
    wge(VE, "S", m_bcsb)
    for th in range(2):
        sl = slice(th * 512, (th + 1) * 512)
        for ch in range(NCH):
            VE.tensor_sub(out=v_t1[:, sl], in0=v_h1[:, ch * T + th * 512: ch * T + th * 512 + 512],
                          in1=v_mb[:, sl])
            i = VE.tensor_mul(out=v_xgT[:, ch * T + th * 512: ch * T + th * 512 + 512],
                              in0=v_t1[:, sl], in1=v_ib[:, sl])
            m_xg_chunk[(ch, th)] = inc(i, "V")
    m_xg_done = cnt["V"]


    p_gate = pv(0, 128, T)
    p_up = pv(1024, 128, T)
    p_down = pv(2048, 128, T)
    p_rs = pv(3072, 1, T)
    m_silu = 0
    m_silu_th = {}
    m_upev_th = {}
    m_gu = {}
    m_gu_hist = {}
    pending_rs = []
    m_downmm = {}
    m_stt = {}
    m_stt_th = {}
    m_stt_halves = {}
    m_corr = 0
    wd_gidx = {}
    for q in range(NQUART):
        for lhc in range(11):
            hc = q * 11 + lhc
            (gg, rg) = add_ring_block(d_wg[hc])
            (gu_, ru) = add_ring_block(d_wu[hc])
            if q == 0:
                wdw = m_womm[(NCH - 1, 1)]
            else:
                wdw = m_downmm[(q - 1, 15)]
            wd_gidx.setdefault(q, []).append(ndma[0])
            ndma[0] += 1
            wload.append(dict(dst=wd_tiles[lhc], src=d_wd[hc], wait=("pe", wdw)))
            wwl(TE, gg)
            wwl(TE, gu_)
            if q == 0 and lhc == 0:
                wge(TE, "V", m_var)
            m_gate_th = {}
            m_up_th = {}
            for th in range(2):
                if m_silu_th.get(th):
                    wge(TE, "S", m_silu_th[th])
                for ch in range(NCH):
                    if q == 0 and lhc == 0:
                        warm(1, off=2048)
                        wge(TE, "V", m_xg_chunk[(ch, th)])
                    i = TE.matmul(p_gate[:, th * 512:(th + 1) * 512],
                                  ring[rg % RING][:, ch * 128:(ch + 1) * 128],
                                  v_xgT[:, ch * T + th * 512: ch * T + th * 512 + 512],
                                  start=(ch == 0), stop=(ch == NCH - 1))
                m_gate_th[th] = inc(i, "PE")
            for th in range(2):
                if m_upev_th.get(th):
                    wge(TE, "S", m_upev_th[th])
                for ch in range(NCH):
                    i = TE.matmul(p_up[:, th * 512:(th + 1) * 512],
                                  ring[ru % RING][:, ch * 128:(ch + 1) * 128],
                                  v_xgT[:, ch * T + th * 512: ch * T + th * 512 + 512],
                                  start=(ch == 0), stop=(ch == NCH - 1))
                m_up_th[th] = inc(i, "PE")
            ring_consumed_mark[rg] = ring_consumed_mark[ru] = m_up_th[1]
            spar = lhc % 2
            v_silu = v_silu2[spar]
            v_upt = v_upt2[spar]
            if lhc >= 2:
                wge(SE, "V", m_gu_hist[(q, lhc - 2)])
            elif q > 0:
                wge(SE, "V", m_gu_hist[(q - 1, 10 - lhc)])
            for th in range(2):
                sl = slice(th * 512, (th + 1) * 512)
                wge(SE, "PE", m_gate_th[th])
                i = SE.activation(out=v_silu[:, sl], in_=p_gate[:, sl], func=AF.Silu,
                                  scale=bcol(BC_SG + hc))
                m_silu_th[th] = inc(i, "S")
            for th in range(2):
                sl = slice(th * 512, (th + 1) * 512)
                wge(SE, "PE", m_up_th[th])
                i = SE.mul(out=v_upt[:, sl], in_=p_up[:, sl], mul=bcol(BC_SU + hc))
                m_upev_th[th] = inc(i, "S")
            m_silu = m_upev_th[1]
            wge(VE, "S", m_silu)
            if q > 0:
                wge(VE, "PE", m_downmm.get((q - 1, 15), 0))
            i = VE.tensor_mul(out=v_gu[lhc], in0=v_silu, in1=v_upt)
            m_gu[(q, lhc)] = inc(i, "V")
            m_gu_hist[(q, lhc)] = m_gu[(q, lhc)]
            pending_rs.append((hc, lhc))
            if len(pending_rs) > 2:
                phc, plhc = pending_rs.pop(0)
                wge(TE, "V", m_gu[(q, plhc)])
                for th in range(2):
                    i = TE.matmul(p_rs[:, th * 512:(th + 1) * 512], v_onec,
                                  v_gu[plhc][:, th * 512:(th + 1) * 512],
                                  start=(phc == 0), stop=False)
                inc(i, "PE")
        while pending_rs:
            phc, plhc = pending_rs.pop(0)
            wge(TE, "V", m_gu[(q, plhc)])
            for th in range(2):
                i = TE.matmul(p_rs[:, th * 512:(th + 1) * 512], v_onec,
                              v_gu[plhc][:, th * 512:(th + 1) * 512],
                              start=(phc == 0), stop=(phc == NHC - 1))
            inc(i, "PE")
        m_rs_done = cnt["PE"]
        if q == NQUART - 1:
            wge(VE, "PE", m_rs_done)
            i = VE.tensor_scalar(out=v_corr, in0=p_rs[0:1, :], scalar1=-128.0,
                                 scalar2=None, op0=ALU.mult)
            m_corr = inc(i, "V")
        for cc in range(NCH):
            for _gi in wd_gidx[q]:
                wwl(TE, _gi)
            last_q = (q == NQUART - 1)
            if last_q:
                wge(TE, "V", m_corr)
            for th in range(2):
                sl = slice(th * 512, (th + 1) * 512)
                if m_stt_th.get(th):
                    wge(TE, "V", m_stt_th[th])
                for lhc in range(11):
                    i = TE.matmul(p_down[:, sl],
                                  wd_tiles[lhc][:, cc * 128:(cc + 1) * 128],
                                  v_gu[lhc][:, sl],
                                  start=(lhc == 0), stop=(lhc == 10 and not last_q))
                if last_q:
                    i = TE.matmul(p_down[:, sl], v_or_bf, v_corr[:, sl],
                                  start=False, stop=True)
                m_down_th = inc(i, "PE")
                wge(VE, "PE", m_down_th)
                i = VE.scalar_tensor_tensor(out=v_h1[:, cc * T + th * 512: cc * T + th * 512 + 512],
                                            in0=p_down[:, sl],
                                            scalar=bcol(BC_SD + cc),
                                            in1=v_h1[:, cc * T + th * 512: cc * T + th * 512 + 512],
                                            op0=ALU.mult, op1=ALU.add)
                m_stt_th[th] = inc(i, "V")
                m_stt_halves[(q, cc, th)] = m_stt_th[th]
            m_downmm[(q, cc)] = m_down_th
            m_stt[(q, cc)] = m_stt_th[1]

    m_final = dict(m_stt)

    wge(SY, "B", 3)
    for cc in range(NCH):
        k = IX_XC0 + NCH + cc
        if cc >= 2:
            wge(SY, "V", m_woe[cc - 2])
        SY.wait_ge(sXp[k % 8], 16 * (k // 8))
        i = SY.dma_start(out=v_xsr[cc % 2], in_=d_xs[:, cc * T:(cc + 1) * T])
        i.then_inc(sXp[k % 8], 16)
    for cc in range(NCH):
        for th in range(2):
            wge(SY, "V", m_stt_halves[(NQUART - 1, cc, th)])
            sl = slice(cc * T + th * 512, cc * T + th * 512 + 512)
            i = SY.dma_start(out=d_y[:, sl], in_=v_h1[:, sl])
            inc(i, "O", 16)
    SY.wait_ge(sO, 16 * NCH * 2)

    gdma = 0
    for blk in wload:
        if blk.get("kind") == "ropesq":
            continue
        gi = gdma
        gdma += 1
        w = blk["wait"]
        if w is not None:
            kind, val = w
            if kind == "ring":
                rseq = val
                if rseq - RING >= 0:
                    GP.wait_ge(sPE, ring_consumed_mark[rseq - RING])
            elif kind == "aring":
                rseq = val
                if rseq - 8 >= 0:
                    GP.wait_ge(sPE, aring_consumed_mark[rseq - 8])
            elif kind == "pe":
                GP.wait_ge(sPE, val)
        if gi >= 8:
            GP.wait_ge(sWp[gi % 8], 16 * (gi // 8))
        i = GP.dma_start(out=blk["dst"], in_=blk["src"])
        i.then_inc(sWp[gi % 8], 16)

    ctx.close()
    _BUILD_CACHE["nc"] = nc
    return nc


def _prep_shared(inputs):
    Wq = np.ascontiguousarray(inputs["L0_attn_Wq"], dtype=np.float32)
    Wk = np.ascontiguousarray(inputs["L0_attn_Wk"], dtype=np.float32)
    Wv = np.ascontiguousarray(inputs["L0_attn_Wv"], dtype=np.float32)
    Wo = np.ascontiguousarray(inputs["L0_attn_Wo"], dtype=np.float32)
    Rm = np.ascontiguousarray(inputs["L0_attn_rand_matrix"], dtype=np.float32)
    wg = np.ascontiguousarray(inputs["L0_ffn_wq_gate"], dtype=np.int32)
    wu = np.ascontiguousarray(inputs["L0_ffn_wq_up"], dtype=np.int32)
    wd = np.ascontiguousarray(inputs["L0_ffn_wq_down"], dtype=np.int32)

    def pair_blocks(W, npair):
        blocks = np.empty((2 * npair, 128, 2048), dtype=W.dtype)
        for P in range(npair):
            for half in range(2):
                cols = np.concatenate([
                    np.arange((2 * P) * 128 + half * 64, (2 * P) * 128 + half * 64 + 64),
                    np.arange((2 * P + 1) * 128 + half * 64, (2 * P + 1) * 128 + half * 64 + 64)])
                sel = W[cols, :]
                t = sel.T.reshape(16, 128, 128)
                blocks[2 * P + half] = t.transpose(1, 0, 2).reshape(128, 2048)
        return np.ascontiguousarray(blocks)

    def plane_blocks(W, nplane):
        blocks = np.empty((nplane, 128, 2048), dtype=W.dtype)
        for j in range(nplane):
            sel = W[j * 128:(j + 1) * 128, :]
            t = sel.T.reshape(16, 128, 128)
            blocks[j] = t.transpose(1, 0, 2).reshape(128, 2048)
        return np.ascontiguousarray(blocks)

    def _u8(blocks):
        return np.ascontiguousarray(blocks.view(np.uint8)[:, :, 0::4])

    def down_blocks(W):
        blocks = np.empty((NHC, 128, 2048), dtype=W.dtype)
        for hc in range(NHC):
            sel = W[:, hc * 128:(hc + 1) * 128]
            t = sel.T
            blocks[hc] = t
        return np.ascontiguousarray(blocks)

    rts = np.empty((128, 128), dtype=np.float32)
    p = np.arange(128)
    rts[:, 0:64] = Rm[:, p % 64].T
    rts[:, 64:128] = Rm[:, 64 + (p % 64)].T

    bias = np.zeros((128, 160), dtype=np.float32)
    bq = np.asarray(inputs["L0_attn_bq"], dtype=np.float32)
    bk = np.asarray(inputs["L0_attn_bk"], dtype=np.float32)
    bv = np.asarray(inputs["L0_attn_bv"], dtype=np.float32)
    bo = np.asarray(inputs["L0_attn_bo"], dtype=np.float32)
    for P in range(8):
        for half in range(2):
            col = np.concatenate([bq[(2 * P) * 128 + half * 64:(2 * P) * 128 + half * 64 + 64],
                                  bq[(2 * P + 1) * 128 + half * 64:(2 * P + 1) * 128 + half * 64 + 64]])
            bias[:, BC_BQ + 2 * P + half] = col
    for P in range(4):
        for half in range(2):
            col = np.concatenate([bk[(2 * P) * 128 + half * 64:(2 * P) * 128 + half * 64 + 64],
                                  bk[(2 * P + 1) * 128 + half * 64:(2 * P + 1) * 128 + half * 64 + 64]])
            bias[:, BC_BK + 2 * P + half] = col
    for j in range(8):
        bias[:, BC_BV + j] = bv[j * 128:(j + 1) * 128]
    for cc in range(16):
        bias[:, BC_BO + cc] = bo[cc * 128:(cc + 1) * 128]
    sg = np.asarray(inputs["L0_ffn_s_gate"], dtype=np.float32)
    su = np.asarray(inputs["L0_ffn_s_up"], dtype=np.float32)
    sd = np.asarray(inputs["L0_ffn_s_down"], dtype=np.float32)
    for hc in range(NHC):
        bias[:, BC_SG + hc] = sg[hc * 128:(hc + 1) * 128]
        bias[:, BC_SU + hc] = su[hc * 128:(hc + 1) * 128]
    for cc in range(16):
        bias[:, BC_SD + cc] = sd[cc * 128:(cc + 1) * 128]
    bias[:, BC_CMAG] = CMAG
    bias[64:128, BC_B025] = 0.25
    bias[64:128, BC_B2] = PI / 2
    bias[:, BC_EPS] = 1e-5
    bias[:, BC_EPS12] = 1e-12

    cbf = np.zeros((128, 520), dtype=ml_dtypes.bfloat16)
    cbf[:, CB_ID:CB_ID + 128] = np.eye(128, dtype=ml_dtypes.bfloat16)
    cbf[0:64, CB_MA:CB_MA + 128] = 1
    cbf[64:128, CB_MB:CB_MB + 128] = 1
    cbf[:, CB_ONEC] = 1
    cbf[:, CB_OAVG:CB_OAVG + 128] = ml_dtypes.bfloat16(1.0 / 2048.0)

    return dict(
        wqs=pair_blocks(Wq, 8), wks=pair_blocks(Wk, 4),
        wvs=plane_blocks(Wv, 8), wos=plane_blocks(Wo, 16),
        wgs=_u8(plane_blocks(wg, NHC)), wus=_u8(plane_blocks(wu, NHC)),
        wds=_u8(down_blocks(wd)),
        rts=rts, biasblob=bias, cbf=cbf,
        ones_r_bf=np.ones((1, 128), dtype=ml_dtypes.bfloat16),
        ones_r_f32=np.ones((1, 128), dtype=np.float32),
    )


def _prep_core(x, c):
    def fmajor(sl):
        t = sl.transpose(2, 0, 1)
        t = t.reshape(16, 128, 2, 512)
        return np.ascontiguousarray(t.transpose(1, 0, 2, 3).reshape(128, 16 * 1024))

    xc = fmajor(x[:, 512 * c:512 * c + 512, :])
    u = np.arange(512)
    sp = 256 * (u // 32) + 32 * c + (u % 32)
    xs = fmajor(x[:, sp, :])
    inv_freq = 1.0 / (10000.0 ** (np.arange(0, 128, 2, dtype=np.float64) / 128))
    pos = np.arange(512 * c, 512 * c + 512, dtype=np.float64)
    ang = pos[None, :] * inv_freq[np.arange(128) % 64][:, None]
    return dict(xc=xc, xs=xs,
                cosd=np.cos(ang).astype(ml_dtypes.bfloat16),
                sind=np.sin(ang).astype(ml_dtypes.bfloat16)), sp


def kernel(**inputs):
    x = np.ascontiguousarray(np.asarray(inputs["L0_attn_hidden_states"], dtype=np.float32))
    shared = _prep_shared(inputs)
    nc = _build()
    in_maps = []
    sps = []
    for c in range(NCORES):
        per, sp = _prep_core(x, c)
        sps.append(sp)
        m = dict(shared)
        m.update(per)
        in_maps.append(m)
    res = run_bass_kernel_spmd(nc, in_maps, core_ids=list(range(NCORES)))
    y = np.empty((2, 4096, 2048), dtype=np.float32)
    for c in range(NCORES):
        yT = res.results[c]["yT"]
        z = yT.reshape(128, 16, 2, 512).transpose(2, 3, 1, 0).reshape(2, 512, 2048)
        y[0, sps[c]] = z[0]
        y[1, sps[c]] = z[1]
    return y


# version 34
# speedup vs baseline: 1.0246x; 1.0246x over previous
import math
import numpy as np
import ml_dtypes
import concourse.bass as bass
import concourse.mybir as mybir
from concourse.bass_utils import run_bass_kernel_spmd

F32 = mybir.dt.float32
BF16 = mybir.dt.bfloat16
I32 = mybir.dt.int32
U8 = mybir.dt.uint8
AF = mybir.ActivationFunctionType
ALU = mybir.AluOpType

PI = math.pi
CMAG = 1.5 * 2 ** 23
NCORES = 8
T = 1024
NCH = 16
NQH, NKH = 16, 8
NHC = 44
NQUART = 4
RING = 4

ARENA_BYTES = 200704
O_CONST = 0
O_RING = 8192
O_A = 24576
O_B = 57344
O_C = 122880
O_D = 155648
O_E = 188416

OC_COS = 0
OC_SIN = 2048
OC_BIAS = 4096
OC_CBF = 4736
OC_ONESR_BF = 5824
OC_ONESR_F32 = 6080
OC_RTS = 6592
OC_BV64 = 6848
BC_BQ, BC_BK, BC_BV, BC_BO = 0, 16, 24, 32
BC_SG, BC_SU, BC_SD = 48, 92, 136
BC_CMAG, BC_B025, BC_B2, BC_EPS, BC_EPS12 = 152, 153, 154, 155, 156
CB_ID, CB_MA, CB_MB, CB_ONEC, CB_OAVG = 0, 128, 256, 384, 392

_BUILD_CACHE = {}


def _build():
    if "nc" in _BUILD_CACHE:
        return _BUILD_CACHE["nc"]
    nc = bass.Bass("TRN2", target_bir_lowering=False, debug=False)

    d_xc = nc.dram_tensor("xc", [128, NCH * T], F32, kind="ExternalInput").ap()
    d_xs = nc.dram_tensor("xs", [128, NCH * T], F32, kind="ExternalInput").ap()
    d_cos = nc.dram_tensor("cosd", [128, 512], BF16, kind="ExternalInput").ap()
    d_sin = nc.dram_tensor("sind", [128, 512], BF16, kind="ExternalInput").ap()
    d_bias = nc.dram_tensor("biasblob", [128, 160], F32, kind="ExternalInput").ap()
    d_cbf = nc.dram_tensor("cbf", [128, 520], BF16, kind="ExternalInput").ap()
    d_or_bf = nc.dram_tensor("ones_r_bf", [1, 128], BF16, kind="ExternalInput").ap()
    d_or_f = nc.dram_tensor("ones_r_f32", [1, 128], F32, kind="ExternalInput").ap()
    d_rts = nc.dram_tensor("rts", [128, 128], F32, kind="ExternalInput").ap()
    d_wq = nc.dram_tensor("wqs", [NQH, 128, 2048], F32, kind="ExternalInput").ap()
    d_wk = nc.dram_tensor("wks", [NKH, 128, 2048], F32, kind="ExternalInput").ap()
    d_wv = nc.dram_tensor("wvs", [NKH, 128, 2048], F32, kind="ExternalInput").ap()
    d_wo = nc.dram_tensor("wos", [NCH, 128, 2048], F32, kind="ExternalInput").ap()
    d_wg = nc.dram_tensor("wgs", [NHC, 128, 2048], U8, kind="ExternalInput").ap()
    d_wu = nc.dram_tensor("wus", [NHC, 128, 2048], U8, kind="ExternalInput").ap()
    d_wd = nc.dram_tensor("wds", [NHC, 128, 2048], U8, kind="ExternalInput").ap()
    d_y = nc.dram_tensor("yT", [128, NCH * T], F32, kind="ExternalOutput").ap()

    from contextlib import ExitStack
    ctx = ExitStack()
    arena_t = ctx.enter_context(nc.sbuf_tensor("arena", [128, ARENA_BYTES], U8))
    parena_t = ctx.enter_context(nc.psum_tensor("parena", [128, 4096], F32))
    sWp = [ctx.enter_context(nc.semaphore(f"sW{i}")) for i in range(8)]
    sXp = [ctx.enter_context(nc.semaphore(f"sX{i}")) for i in range(8)]
    sPE = ctx.enter_context(nc.semaphore("sPE"))
    sV = ctx.enter_context(nc.semaphore("sV"))
    sS = ctx.enter_context(nc.semaphore("sS"))
    sO = ctx.enter_context(nc.semaphore("sO"))
    sB = ctx.enter_context(nc.semaphore("sB"))
    sG = ctx.enter_context(nc.semaphore("sG"))
    block = ctx.enter_context(nc.Block())

    AR = arena_t[:]
    PA = parena_t[:]

    def cv(off, dt, parts, elems):
        b = AR.bitcast(dt)
        es = {F32: 4, BF16: 2, I32: 4, U8: 1}[dt]
        assert off % es == 0
        return b[0:parts, off // es: off // es + elems]

    def pv(off_elems, parts, elems):
        return PA[0:parts, off_elems: off_elems + elems]

    def warm(n, off=2048):
        dst = PA[0:128, off:off + 512]
        for _ in range(n):
            TE.matmul(dst, v_ident, v_cbf[:, 0:512], start=True, stop=True)

    v_cos = cv(O_CONST + OC_COS, BF16, 128, 512)
    v_sin = cv(O_CONST + OC_SIN, BF16, 128, 512)
    v_bias = cv(O_CONST + OC_BIAS, F32, 128, 160)
    v_cbf = cv(O_CONST + OC_CBF, BF16, 128, 520)
    v_or_bf = cv(O_CONST + OC_ONESR_BF, BF16, 1, 128)
    v_or_f = cv(O_CONST + OC_ONESR_F32, F32, 1, 128)
    v_rts = cv(O_CONST + OC_RTS, BF16, 128, 128)
    v_bv64 = cv(O_CONST + OC_BV64, F32, 128, 8)
    v_ident = v_cbf[:, CB_ID:CB_ID + 128]
    v_maskA = v_cbf[:, CB_MA:CB_MA + 128]
    v_maskB = v_cbf[:, CB_MB:CB_MB + 128]
    v_onec = v_cbf[:, CB_ONEC:CB_ONEC + 1]
    v_oavg = v_cbf[:, CB_OAVG:CB_OAVG + 128]

    def bcol(idx):
        return v_bias[:, idx:idx + 1]

    ring = [cv(O_RING + i * 4096, BF16, 128, 2048) for i in range(RING)]
    aring = [cv(O_D + i * 4096, BF16, 128, 2048) for i in range(8)]
    wd_tiles = [cv((O_D + 4096 * i) if i < 8 else (O_E + 4096 * (i - 8)), BF16, 128, 2048)
                for i in range(11)]

    v_xaT = cv(O_A, BF16, 128, NCH * T)
    v_xsr = [cv(O_A + i * 4096, F32, 128, T) for i in range(2)]
    v_xgT = cv(O_A, BF16, 128, NCH * T)
    v_xc = [cv(O_B + ch * 4096, F32, 128, T) for ch in range(NCH)]
    v_qpT = cv(O_B, BF16, 128, NQH * T)
    v_kpT = cv(O_B + 32768, BF16, 128, NKH * T)
    v_vT = cv(O_B + 49152, BF16, 128, NKH * T)
    v_h1 = cv(O_B, F32, 128, NCH * T)
    v_scr = cv(O_D, BF16, 128, NCH * T)

    v_vsc = cv(O_E, F32, 128, T)
    v_rt2 = cv(O_E + 4096, BF16, 128, T)
    v_rt4 = cv(O_E + 6144, BF16, 128, T)
    v_xbf = [cv(O_C + i * 2048, BF16, 128, T) for i in range(2)]
    v_sqx = [cv(O_C + 4096 + i * 2048, BF16, 128, T) for i in range(2)]
    v_mb = cv(O_C + 8192, F32, 128, T)
    v_ib = cv(O_C + 12288, F32, 128, T)
    v_row = [cv(O_C + 16384 + i * 4096, F32, 1, T) for i in range(3)]
    v_t1 = cv(O_C + 28672, F32, 128, T)
    v_qel2 = [cv(O_C + i * 2048, BF16, 128, T) for i in range(2)]
    v_qeh2 = [cv(O_C + 4096 + i * 2048, BF16, 128, T) for i in range(2)]
    v_qpl2 = [cv(O_C + 8192 + i * 2048, BF16, 128, T) for i in range(2)]
    v_qph2 = [cv(O_C + 12288 + i * 2048, BF16, 128, T) for i in range(2)]
    v_sql = cv(O_C + 16384, BF16, 128, T)
    v_sqh = cv(O_C + 18432, BF16, 128, T)
    v_rn2 = [cv(O_C + 20480 + i * 2048, BF16, 128, T) for i in range(2)]
    v_r0 = cv(O_C + 24576, F32, 128, T)
    v_r2 = cv(O_C + 28672, F32, 128, T)
    v_kpp = cv(O_C, BF16, 128, 4096)
    v_vjp = [cv(O_C + 8192 + i * 2048, BF16, 64, T) for i in range(2)]
    v_a2 = [cv(O_C + 12288 + i * 2048, BF16, 64, T) for i in range(2)]
    v_gu = [cv(O_C + hc * 2048, BF16, 128, T) for hc in range(11)]
    v_silu2 = [cv(O_C + 22528 + i * 2048, BF16, 128, T) for i in range(2)]
    v_upt2 = [cv(O_C + 26624 + i * 2048, BF16, 128, T) for i in range(2)]
    v_corr = cv(O_C + 30720, BF16, 1, T)

    TE, VE, SE, SY, GP = nc.tensor, nc.vector, nc.scalar, nc.sync, nc.gpsimd

    cnt = dict(PE=0, V=0, S=0, O=0, B=0, G=0)
    SEM = dict(PE=sPE, V=sV, S=sS, O=sO, B=sB, G=sG)

    def wxl(eng, i):
        eng.wait_ge(sXp[i % 8], 16 * (i // 8 + 1))

    def wwl(eng, i):
        eng.wait_ge(sWp[i % 8], 16 * (i // 8 + 1))

    def inc(instr, s, n=1):
        cnt[s] += n
        instr.then_inc(SEM[s], n)
        return cnt[s]

    def wge(eng, s, val):
        if val > 0:
            eng.wait_ge(SEM[s], val)

    xload = []
    xload.append((v_cos, d_cos))
    xload.append((v_sin, d_sin))
    xload.append((v_bias, d_bias))
    xload.append((v_cbf, d_cbf))
    xload.append((v_or_bf, d_or_bf))
    xload.append((v_or_f, d_or_f))
    IX_CONST_LAST = 5
    IX_XC0 = 6
    for ch in range(NCH):
        xload.append((v_xc[ch], d_xc[:, ch * T:(ch + 1) * T]))

    wload = []
    wload.append(dict(dst=v_rts, src=d_rts, wait=None))
    W_RT = 0
    ring_blocks = []
    ring_consumed_mark = {}
    aring_consumed_mark = {}

    aring_blocks = []
    ndma = [1]

    def add_ring_block(src, attn=False):
        gi = ndma[0]
        ndma[0] += 1
        if attn:
            rseq = len(aring_blocks)
            wload.append(dict(dst=aring[rseq % 8], src=src, wait=("aring", rseq)))
            aring_blocks.append(gi)
            return gi, rseq
        rseq = len(ring_blocks)
        wload.append(dict(dst=ring[rseq % RING], src=src, wait=("ring", rseq)))
        ring_blocks.append(gi)
        return gi, rseq

    bar_val = [0]

    def barrier():
        for eng in (TE, VE, SE):
            eng.drain()
            eng.sem_inc(sB, 1)
        bar_val[0] += 3
        for eng in (TE, VE, SE):
            wge(eng, "B", bar_val[0])
        cnt["B"] = bar_val[0]

    for k, (dst, src) in enumerate(xload):
        eng = SE if (IX_XC0 + 8 <= k < IX_XC0 + 16) else SY
        if k >= 8:
            eng.wait_ge(sXp[k % 8], 16 * (k // 8))
        i = eng.dma_start(out=dst, in_=src)
        i.then_inc(sXp[k % 8], 16)

    for i in range(IX_CONST_LAST + 1):
        wxl(VE, i)
    VE.tensor_scalar(out=v_bv64, in0=v_bias[:, BC_BV:BC_BV + 8], scalar1=1.0 / 64.0,
                     scalar2=None, op0=ALU.mult)
    m_bv64 = inc(nc.vector.engine_nop(), "V")

    p_mb = pv(0, 128, T)
    p_sqb = pv(1024, 128, T)
    m_statmm = {}
    for _i in range(IX_CONST_LAST + 1):
        wxl(TE, _i)
    for ch in range(NCH):
        buf = ch % 2
        wxl(VE, IX_XC0 + ch)
        if ch >= 2:
            wge(VE, "PE", m_statmm[ch - 2])
        VE.tensor_copy(out=v_xbf[buf], in_=v_xc[ch])
        i = VE.tensor_mul(out=v_sqx[buf], in0=v_xc[ch], in1=v_xc[ch])
        m_cast = inc(i, "V")
        warm(4)
        wge(TE, "V", m_cast)
        for th in range(2):
            TE.matmul(p_mb[:, th * 512:(th + 1) * 512], v_oavg,
                      v_xbf[buf][:, th * 512:(th + 1) * 512], start=(ch == 0), stop=(ch == NCH - 1))
        for th in range(2):
            i = TE.matmul(p_sqb[:, th * 512:(th + 1) * 512], v_oavg,
                          v_sqx[buf][:, th * 512:(th + 1) * 512], start=(ch == 0), stop=(ch == NCH - 1))
        m_statmm[ch] = inc(i, "PE")
    warm(40)
    wge(VE, "PE", m_statmm[NCH - 1])
    VE.tensor_copy(out=v_mb, in_=p_mb)
    VE.tensor_mul(out=v_ib, in0=v_mb, in1=v_mb)
    i = VE.tensor_sub(out=v_ib, in0=p_sqb, in1=v_ib)
    m_var = inc(i, "V")
    wge(SE, "V", m_var)
    for _i in range(IX_CONST_LAST + 1):
        wxl(SE, _i)
    SE.activation(out=v_ib, in_=v_ib, func=AF.Ln, bias=bcol(BC_EPS))
    i = SE.activation(out=v_ib, in_=v_ib, func=AF.Exp, scale=-0.5)
    m_bcsb = inc(i, "S")
    m_xa_chunk = {}
    wge(VE, "S", m_bcsb)
    for th in range(2):
        sl = slice(th * 512, (th + 1) * 512)
        for ch in range(NCH):
            VE.tensor_sub(out=v_t1[:, sl], in0=v_xc[ch][:, sl], in1=v_mb[:, sl])
            i = VE.tensor_mul(out=v_xaT[:, ch * T + th * 512: ch * T + th * 512 + 512],
                              in0=v_t1[:, sl], in1=v_ib[:, sl])
            m_xa_chunk[(ch, th)] = inc(i, "V")
    m_xa_done = cnt["V"]

    p_lo = pv(0, 128, T)
    p_hi = pv(1024, 128, T)
    p_p2 = pv(2048, 128, T)
    p_rnr = pv(3072, 128, T)
    m_r0_prev = 0
    m_clamp_prev = 0
    m_evac_prev = 0
    m_rope_prev = 0
    m_p2_prev = 0
    m_feat_prev = 0
    m_feat = {}
    wwl(TE, W_RT)

    pairs = [("k", p) for p in range(NKH // 2)] + [("q", p) for p in range(NQH // 2)]
    proj_mark = {}
    _evac_prev = [0]

    def emit_proj(pi):
        fam, P = pairs[pi]
        dsrc = d_wq if fam == "q" else d_wk
        (gl, rl) = add_ring_block(dsrc[2 * P + 0], attn=True)
        (gh, rh) = add_ring_block(dsrc[2 * P + 1], attn=True)
        for half, (gidx, rseq, psum) in enumerate(((gl, rl, p_lo), (gh, rh, p_hi))):
            wwl(TE, gidx)
            if _evac_prev[0]:
                wge(TE, "S", _evac_prev[0])
            slot = aring[rseq % 8]
            for th in range(2):
                for ch in range(NCH):
                    i = TE.matmul(psum[:, th * 512:(th + 1) * 512],
                                  slot[:, ch * 128:(ch + 1) * 128],
                                  v_xaT[:, ch * T + th * 512: ch * T + th * 512 + 512],
                                  start=(ch == 0), stop=(ch == NCH - 1))
            aring_consumed_mark[rseq] = inc(i, "PE")
        proj_mark[pi] = cnt["PE"]


    m_rope_hist = {}
    m_p2_hist = {}
    m_mask_hist = {}
    m_evac_hist = {}

    def emit_evac(pi):
        fam, P = pairs[pi]
        par = pi % 2
        bc0 = BC_BQ if fam == "q" else BC_BK
        wge(SE, "PE", proj_mark[pi])
        if pi >= 2:
            wge(SE, "V", m_rope_hist[pi - 2])
        SE.add(out=v_qel2[par], in_=p_lo, add=bcol(bc0 + 2 * P + 0))
        i = SE.add(out=v_qeh2[par], in_=p_hi, add=bcol(bc0 + 2 * P + 1))
        m_evac_hist[pi] = inc(i, "S")
        _evac_prev[0] = m_evac_hist[pi]

    m_vevac = {}

    def emit_vplane(j):
        (gi, rseq) = add_ring_block(d_wv[j], attn=True)
        wwl(TE, gi)
        if m_vevac.get(j - 2):
            wge(TE, "S", m_vevac[j - 2])
        psum = p_lo if j % 2 == 0 else p_hi
        slot = aring[rseq % 8]
        for th in range(2):
            for ch in range(NCH):
                if j == 0:
                    warm(1, off=3072)
                    wge(TE, "V", m_xa_chunk[(ch, th)])
                i = TE.matmul(psum[:, th * 512:(th + 1) * 512],
                              slot[:, ch * 128:(ch + 1) * 128],
                              v_xaT[:, ch * T + th * 512: ch * T + th * 512 + 512],
                              start=(ch == 0), stop=(ch == NCH - 1))
        aring_consumed_mark[rseq] = inc(i, "PE")
        m_vmm = cnt["PE"]
        wge(SE, "PE", m_vmm)
        wge(SE, "V", m_bv64)
        SE.mul(out=v_vsc, in_=psum, mul=1.0 / 64.0)
        vdst = v_vT.rearrange("p (t j) -> p j t", j=NKH)[:, j:j + 1, :]
        i = SE.add(out=vdst, in_=v_vsc, add=v_bv64[:, j:j + 1])
        m_vevac[j] = inc(i, "S")
        _evac_prev[0] = m_vevac[j]

    for _j in range(NKH):
        emit_vplane(_j)
    emit_proj(0)
    emit_evac(0)
    for pi, (fam, P) in enumerate(pairs):
        par = pi % 2
        v_qel = v_qel2[par]
        v_qeh = v_qeh2[par]
        v_qpl = v_qpl2[par]
        v_qph = v_qph2[par]
        dstT = v_qpT if fam == "q" else v_kpT
        bc0 = BC_BQ if fam == "q" else BC_BK
        m_evac_prev = m_evac_hist[pi]
        if pi + 1 < len(pairs):
            emit_proj(pi + 1)
        wge(VE, "S", m_evac_prev)
        if pi >= 1:
            wge(VE, "PE", m_mask_hist[pi - 1])
        VE.tensor_mul(out=v_sql, in0=v_qel, in1=v_qel)
        i = VE.tensor_mul(out=v_sqh, in0=v_qeh, in1=v_qeh)
        m_sq = inc(i, "V")
        if pi >= 2:
            wge(VE, "PE", m_p2_hist[pi - 2])
        for bh in range(2):
            sl = slice(bh * 512, (bh + 1) * 512)
            VE.tensor_mul(out=v_rt2[:, sl], in0=v_qeh[:, sl], in1=v_sin)
            VE.tensor_mul(out=v_rt4[:, sl], in0=v_qel[:, sl], in1=v_sin)
            VE.tensor_mul(out=v_qel[:, sl], in0=v_qel[:, sl], in1=v_cos)
            VE.tensor_mul(out=v_qeh[:, sl], in0=v_qeh[:, sl], in1=v_cos)
            VE.tensor_sub(out=v_qpl[:, sl], in0=v_qel[:, sl], in1=v_rt2[:, sl])
            i = VE.tensor_add(out=v_qph[:, sl], in0=v_qeh[:, sl], in1=v_rt4[:, sl])
        m_rope = inc(i, "V")
        m_rope_hist[pi] = m_rope
        m_rn_h = {}
        for hh in range(2):
            mask = v_maskA if hh == 0 else v_maskB
            rn = v_rn2[hh]
            wge(TE, "V", m_sq)
            if m_clamp_prev:
                wge(TE, "S", m_clamp_prev)
            for th in range(2):
                sl = slice(th * 512, (th + 1) * 512)
                TE.matmul(p_rnr[:, sl], mask, v_sql[:, sl], start=True, stop=False)
                i = TE.matmul(p_rnr[:, sl], mask, v_sqh[:, sl], start=False, stop=True)
            m_maskmm_last = inc(i, "PE")
            m_mask_hist[pi] = m_maskmm_last
            wge(SE, "PE", m_maskmm_last)
            SE.activation(out=p_rnr, in_=p_rnr, func=AF.Ln, bias=bcol(BC_EPS12))
            i = SE.activation(out=rn, in_=p_rnr, func=AF.Exp, scale=-0.5)
            m_rn_h[hh] = inc(i, "S")
            m_clamp_prev = m_rn_h[hh]
        if pi + 1 < len(pairs):
            emit_evac(pi + 1)
        for hh in range(2):
            head = 2 * P + hh
            beta = 64 * hh
            rn = v_rn2[hh]
            wge(TE, "V", m_rope)
            if m_r0_prev:
                wge(TE, "V", m_r0_prev)
            for dbeta in (0, 64):
                for th in range(2):
                    sl = slice(th * 512, (th + 1) * 512)
                    TE.matmul(p_p2[dbeta:dbeta + 64, sl], v_rts[beta:beta + 64, 0:64],
                              v_qpl[beta:beta + 64, sl], start=True, stop=False)
                    i = TE.matmul(p_p2[dbeta:dbeta + 64, sl], v_rts[beta:beta + 64, 64:128],
                                  v_qph[beta:beta + 64, sl], start=False, stop=True)
            m_p2 = inc(i, "PE")
            m_p2_hist[pi] = m_p2
            wge(VE, "PE", m_p2)
            wge(VE, "S", m_rn_h[hh])
            if m_feat_prev:
                wge(VE, "S", m_feat_prev)
            i = VE.tensor_mul(out=v_r0, in0=p_p2, in1=rn)
            m_r0_prev = inc(i, "V")
            VE.tensor_scalar(out=v_r2, in0=v_r0, scalar1=1.0 / (2 * PI),
                             scalar2=bcol(BC_B025), op0=ALU.mult, op1=ALU.add)
            VE.tensor_scalar(out=v_r2, in0=v_r2, scalar1=CMAG, scalar2=CMAG,
                             op0=ALU.add, op1=ALU.subtract)
            i = VE.scalar_tensor_tensor(out=v_r2, in0=v_r2, scalar=-2 * PI,
                                        in1=v_r0, op0=ALU.mult, op1=ALU.add)
            m_w = inc(i, "V")
            wge(SE, "V", m_w)
            i = SE.activation(out=dstT[:, head * T:(head + 1) * T], in_=v_r2,
                              func=AF.Sin, bias=bcol(BC_B2))
            m_feat[(fam, head)] = inc(i, "S")
            m_feat_prev = m_feat[(fam, head)]
            m_p2_prev = m_p2
        m_rope_prev = m_rope

    p_a2 = [pv(0, 64, T), pv(1024, 64, T)]
    p_out2 = [pv(2048, 128, T), pv(3072, 128, T)]
    p_vjp2 = [PA[64:128, 0:1024], PA[64:128, 1024:2048]]
    m_kpp = {}
    m_a2ev = {}
    m_vjpev = {}
    m_mm1 = {}
    m_mm2 = {}
    m_vjpmm = {}
    m_screvac = {}

    def emit_kppad(mt):
        t0k = mt * 64
        wge(SE, "V", m_ms)
        if mt >= 1:
            wge(SE, "PE", m_mm1[mt - 1])
        for r4 in range(4):
            for par in range(2):
                tau0 = r4 + 4 * par
                src = v_kpT.rearrange("p (j t) -> p j t", j=NKH)[:, :, t0k + tau0::8][:, :, 0:8]
                src = src.rearrange("p j t -> p t j")
                dst = v_kpp.rearrange("p (t s) -> p t s", s=64)[:, tau0::8,
                                                               32 * par + r4 * 8: 32 * par + r4 * 8 + 8]
                i = SE.copy(out=dst, in_=src)
        m_kpp[mt] = inc(i, "S")

    def emit_vjpmm(m):
        if m >= 2:
            wge(TE, "V", m_vjpev[m - 2])
        dst = p_vjp2[m % 2]
        t0m = m * 64
        for g8 in range(8):
            gpos = t0m + g8 * 8
            lhsT = v_vT[:, gpos * 8: gpos * 8 + 64]
            i = TE.matmul(dst[:, g8 * 128:(g8 + 1) * 128], lhsT, v_ident,
                          start=True, stop=True)
        m_vjpmm[m] = inc(i, "PE")

    i = VE.memset(v_kpp, 0.0)
    m_ms = inc(i, "V")
    emit_kppad(0)
    wge(TE, "S", m_evac_hist[len(pairs) - 1])
    emit_vjpmm(0)

    barrier()

    for mt in range(16):
        ab = mt % 2
        t0 = mt * 64
        if mt + 1 < 16:
            emit_vjpmm(mt + 1)
        wge(VE, "PE", m_vjpmm[mt])
        if mt >= 2:
            wge(VE, "PE", m_mm2[mt - 2])
        i = VE.tensor_copy(out=v_vjp[ab], in_=p_vjp2[mt % 2])
        m_vjpev[mt] = inc(i, "V")
        wge(TE, "S", m_kpp[mt])
        if mt >= 2:
            wge(TE, "V", m_a2ev[mt - 2])
        qpTh = v_qpT.rearrange("p (h t) -> p t h", h=NQH)
        m_mm1h = {}
        for half in range(2):
            for tau in range(half * 32, half * 32 + 32):
                qslice = qpTh[:, t0 + tau:t0 + tau + 1, :].rearrange("p t h -> p (t h)")
                i = TE.matmul(p_a2[ab][:, tau * 16:(tau + 1) * 16],
                              v_kpp[:, tau * 64:(tau + 1) * 64], qslice,
                              start=True, stop=True)
            m_mm1h[half] = inc(i, "PE")
        m_mm1[mt] = m_mm1h[1]
        m_a2evh = {}
        for half in range(2):
            wge(VE, "PE", m_mm1h[half])
            if mt >= 2:
                wge(VE, "PE", m_mm2[mt - 2])
            sl = slice(half * 512, half * 512 + 512)
            i = VE.tensor_copy(out=v_a2[ab][:, sl], in_=p_a2[ab][:, sl])
            m_a2evh[half] = inc(i, "V")
        m_a2ev[mt] = m_a2evh[1]
        wge(TE, "V", m_vjpev[mt])
        if mt >= 2:
            wge(TE, "V", m_screvac[mt - 2])
        for g8 in range(8):
            wge(TE, "V", m_a2evh[g8 // 4])
            i = TE.matmul(p_out2[mt % 2][:, g8 * 128:(g8 + 1) * 128],
                          v_vjp[ab][:, g8 * 128:(g8 + 1) * 128],
                          v_a2[ab][:, g8 * 128:(g8 + 1) * 128],
                          start=True, stop=True)
        m_mm2[mt] = inc(i, "PE")
        if mt + 1 < 16:
            emit_kppad(mt + 1)
        wge(VE, "PE", m_mm2[mt])
        b = mt // 8
        src = p_out2[mt % 2].rearrange("p (r j h) -> p j h r", r=4, j=16, h=16)
        dst = v_scr.rearrange("p (j b h r) -> p b j h r", j=16, b=2, h=16, r=32)
        dst = dst[:, b:b + 1, :, :, (mt % 8) * 4:(mt % 8) * 4 + 4]
        dst = dst.rearrange("p b j h r -> p (b j) h r")
        i = VE.tensor_copy(out=dst, in_=src)
        m_screvac[mt] = inc(i, "V")

    barrier()

    p_wo = [pv(0, 128, T), pv(1024, 128, T)]
    p_m2 = pv(2048, 128, T)
    p_sq2 = pv(3072, 128, T)
    m_woe = {}
    m_woe_th = {}
    m_womm = {}
    m_statmm2 = {}
    m_cast2 = {}

    def emit_stat2(cc):
        buf = cc % 2
        wge(TE, "V", m_cast2[cc])
        for th in range(2):
            TE.matmul(p_m2[:, th * 512:(th + 1) * 512], v_oavg,
                      v_xbf[buf][:, th * 512:(th + 1) * 512], start=(cc == 0), stop=(cc == NCH - 1))
        for th in range(2):
            i = TE.matmul(p_sq2[:, th * 512:(th + 1) * 512], v_oavg,
                          v_sqx[buf][:, th * 512:(th + 1) * 512], start=(cc == 0), stop=(cc == NCH - 1))
        m_statmm2[cc] = inc(i, "PE")
    for cc in range(NCH):
        (gi, rseq) = add_ring_block(d_wo[cc])
        wwl(TE, gi)
        slot = ring[rseq % RING]
        for th in range(2):
            sl = slice(th * 512, (th + 1) * 512)
            if m_woe_th.get((cc % 2, th)):
                wge(TE, "V", m_woe_th[(cc % 2, th)])
            for ch in range(NCH):
                i = TE.matmul(p_wo[cc % 2][:, sl],
                              slot[:, ch * 128:(ch + 1) * 128],
                              v_scr[:, ch * T + th * 512: ch * T + th * 512 + 512],
                              start=(ch == 0), stop=(ch == NCH - 1))
            m_womm[(cc, th)] = inc(i, "PE")
            wge(VE, "PE", m_womm[(cc, th)])
            wxl(VE, IX_XC0 + NCH + cc)
            i = VE.scalar_tensor_tensor(out=v_h1[:, cc * T + th * 512: cc * T + th * 512 + 512],
                                        in0=p_wo[cc % 2][:, sl],
                                        scalar=bcol(BC_BO + cc), in1=v_xsr[cc % 2][:, sl],
                                        op0=ALU.add, op1=ALU.add)
            m_woe_th[(cc % 2, th)] = inc(i, "V")
        ring_consumed_mark[rseq] = m_womm[(cc, 1)]
        m_woe[cc] = m_woe_th[(cc % 2, 1)]
        buf = cc % 2
        if cc >= 2:
            wge(VE, "PE", m_statmm2[cc - 2])
        h1c = v_h1[:, cc * T:(cc + 1) * T]
        VE.tensor_copy(out=v_xbf[buf], in_=h1c)
        i = VE.tensor_mul(out=v_sqx[buf], in0=h1c, in1=h1c)
        m_cast2[cc] = inc(i, "V")
        if cc >= 1:
            emit_stat2(cc - 1)
        if cc == NCH - 1:
            emit_stat2(cc)

    barrier()

    warm(40, off=0)
    wge(VE, "PE", m_statmm2[NCH - 1])
    VE.tensor_copy(out=v_mb, in_=p_m2)
    VE.tensor_mul(out=v_ib, in0=v_mb, in1=v_mb)
    i = VE.tensor_sub(out=v_ib, in0=p_sq2, in1=v_ib)
    m_var = inc(i, "V")
    wge(SE, "V", m_var)
    SE.activation(out=v_ib, in_=v_ib, func=AF.Ln, bias=bcol(BC_EPS))
    i = SE.activation(out=v_ib, in_=v_ib, func=AF.Exp, scale=-0.5)
    m_bcsb = inc(i, "S")
    m_xg_chunk = {}
    wge(VE, "S", m_bcsb)
    for th in range(2):
        sl = slice(th * 512, (th + 1) * 512)
        for ch in range(NCH):
            VE.tensor_sub(out=v_t1[:, sl], in0=v_h1[:, ch * T + th * 512: ch * T + th * 512 + 512],
                          in1=v_mb[:, sl])
            i = VE.tensor_mul(out=v_xgT[:, ch * T + th * 512: ch * T + th * 512 + 512],
                              in0=v_t1[:, sl], in1=v_ib[:, sl])
            m_xg_chunk[(ch, th)] = inc(i, "V")
    m_xg_done = cnt["V"]


    p_gate = pv(0, 128, T)
    p_up = pv(1024, 128, T)
    p_down = pv(2048, 128, T)
    p_rs = pv(3072, 1, T)
    m_silu = 0
    m_silu_th = {}
    m_upev_th = {}
    m_gu = {}
    m_gu_hist = {}
    pending_rs = []
    m_downmm = {}
    m_stt = {}
    m_stt_th = {}
    m_stt_halves = {}
    m_corr = 0
    wd_gidx = {}
    for q in range(NQUART):
        for lhc in range(11):
            hc = q * 11 + lhc
            (gg, rg) = add_ring_block(d_wg[hc])
            (gu_, ru) = add_ring_block(d_wu[hc])
            if q == 0:
                wdw = m_womm[(NCH - 1, 1)]
            else:
                wdw = m_downmm[(q - 1, 15)]
            wd_gidx.setdefault(q, []).append(ndma[0])
            ndma[0] += 1
            wload.append(dict(dst=wd_tiles[lhc], src=d_wd[hc], wait=("pe", wdw)))
            wwl(TE, gg)
            wwl(TE, gu_)
            if q == 0 and lhc == 0:
                wge(TE, "V", m_var)
            m_gate_th = {}
            m_up_th = {}
            for th in range(2):
                if m_silu_th.get(th):
                    wge(TE, "S", m_silu_th[th])
                for ch in range(NCH):
                    if q == 0 and lhc == 0:
                        warm(1, off=2048)
                        wge(TE, "V", m_xg_chunk[(ch, th)])
                    i = TE.matmul(p_gate[:, th * 512:(th + 1) * 512],
                                  ring[rg % RING][:, ch * 128:(ch + 1) * 128],
                                  v_xgT[:, ch * T + th * 512: ch * T + th * 512 + 512],
                                  start=(ch == 0), stop=(ch == NCH - 1))
                m_gate_th[th] = inc(i, "PE")
            for th in range(2):
                if m_upev_th.get(th):
                    wge(TE, "S", m_upev_th[th])
                for ch in range(NCH):
                    i = TE.matmul(p_up[:, th * 512:(th + 1) * 512],
                                  ring[ru % RING][:, ch * 128:(ch + 1) * 128],
                                  v_xgT[:, ch * T + th * 512: ch * T + th * 512 + 512],
                                  start=(ch == 0), stop=(ch == NCH - 1))
                m_up_th[th] = inc(i, "PE")
            ring_consumed_mark[rg] = ring_consumed_mark[ru] = m_up_th[1]
            spar = lhc % 2
            v_silu = v_silu2[spar]
            v_upt = v_upt2[spar]
            if lhc >= 2:
                wge(SE, "V", m_gu_hist[(q, lhc - 2)])
            elif q > 0:
                wge(SE, "V", m_gu_hist[(q - 1, 10 - lhc)])
            for th in range(2):
                sl = slice(th * 512, (th + 1) * 512)
                wge(SE, "PE", m_gate_th[th])
                i = SE.activation(out=v_silu[:, sl], in_=p_gate[:, sl], func=AF.Silu,
                                  scale=bcol(BC_SG + hc))
                m_silu_th[th] = inc(i, "S")
            for th in range(2):
                sl = slice(th * 512, (th + 1) * 512)
                wge(SE, "PE", m_up_th[th])
                i = SE.mul(out=v_upt[:, sl], in_=p_up[:, sl], mul=bcol(BC_SU + hc))
                m_upev_th[th] = inc(i, "S")
            m_silu = m_upev_th[1]
            wge(VE, "S", m_silu)
            if q > 0:
                wge(VE, "PE", m_downmm.get((q - 1, 15), 0))
            i = VE.tensor_mul(out=v_gu[lhc], in0=v_silu, in1=v_upt)
            m_gu[(q, lhc)] = inc(i, "V")
            m_gu_hist[(q, lhc)] = m_gu[(q, lhc)]
            pending_rs.append((hc, lhc))
            if len(pending_rs) > 2:
                phc, plhc = pending_rs.pop(0)
                wge(TE, "V", m_gu[(q, plhc)])
                for th in range(2):
                    i = TE.matmul(p_rs[:, th * 512:(th + 1) * 512], v_onec,
                                  v_gu[plhc][:, th * 512:(th + 1) * 512],
                                  start=(phc == 0), stop=False)
                inc(i, "PE")
        while pending_rs:
            phc, plhc = pending_rs.pop(0)
            wge(TE, "V", m_gu[(q, plhc)])
            for th in range(2):
                i = TE.matmul(p_rs[:, th * 512:(th + 1) * 512], v_onec,
                              v_gu[plhc][:, th * 512:(th + 1) * 512],
                              start=(phc == 0), stop=(phc == NHC - 1))
            inc(i, "PE")
        m_rs_done = cnt["PE"]
        if q == NQUART - 1:
            wge(VE, "PE", m_rs_done)
            i = VE.tensor_scalar(out=v_corr, in0=p_rs[0:1, :], scalar1=-128.0,
                                 scalar2=None, op0=ALU.mult)
            m_corr = inc(i, "V")
        for cc in range(NCH):
            for _gi in wd_gidx[q]:
                wwl(TE, _gi)
            last_q = (q == NQUART - 1)
            if last_q:
                wge(TE, "V", m_corr)
            for th in range(2):
                sl = slice(th * 512, (th + 1) * 512)
                if m_stt_th.get(th):
                    wge(TE, "V", m_stt_th[th])
                for lhc in range(11):
                    i = TE.matmul(p_down[:, sl],
                                  wd_tiles[lhc][:, cc * 128:(cc + 1) * 128],
                                  v_gu[lhc][:, sl],
                                  start=(lhc == 0), stop=(lhc == 10 and not last_q))
                if last_q:
                    i = TE.matmul(p_down[:, sl], v_or_bf, v_corr[:, sl],
                                  start=False, stop=True)
                m_down_th = inc(i, "PE")
                wge(VE, "PE", m_down_th)
                i = VE.scalar_tensor_tensor(out=v_h1[:, cc * T + th * 512: cc * T + th * 512 + 512],
                                            in0=p_down[:, sl],
                                            scalar=bcol(BC_SD + cc),
                                            in1=v_h1[:, cc * T + th * 512: cc * T + th * 512 + 512],
                                            op0=ALU.mult, op1=ALU.add)
                m_stt_th[th] = inc(i, "V")
                m_stt_halves[(q, cc, th)] = m_stt_th[th]
            m_downmm[(q, cc)] = m_down_th
            m_stt[(q, cc)] = m_stt_th[1]

    m_final = dict(m_stt)

    wge(SY, "B", 3)
    for cc in range(NCH):
        k = IX_XC0 + NCH + cc
        if cc >= 2:
            wge(SY, "V", m_woe[cc - 2])
        SY.wait_ge(sXp[k % 8], 16 * (k // 8))
        i = SY.dma_start(out=v_xsr[cc % 2], in_=d_xs[:, cc * T:(cc + 1) * T])
        i.then_inc(sXp[k % 8], 16)
    for cc in range(NCH):
        for th in range(2):
            wge(SY, "V", m_stt_halves[(NQUART - 1, cc, th)])
            sl = slice(cc * T + th * 512, cc * T + th * 512 + 512)
            i = SY.dma_start(out=d_y[:, sl], in_=v_h1[:, sl])
            inc(i, "O", 16)
    SY.wait_ge(sO, 16 * NCH * 2)

    gdma = 0
    for blk in wload:
        if blk.get("kind") == "ropesq":
            continue
        gi = gdma
        gdma += 1
        w = blk["wait"]
        if w is not None:
            kind, val = w
            if kind == "ring":
                rseq = val
                if rseq - RING >= 0:
                    GP.wait_ge(sPE, ring_consumed_mark[rseq - RING])
            elif kind == "aring":
                rseq = val
                if rseq - 8 >= 0:
                    GP.wait_ge(sPE, aring_consumed_mark[rseq - 8])
            elif kind == "pe":
                GP.wait_ge(sPE, val)
        if gi >= 8:
            GP.wait_ge(sWp[gi % 8], 16 * (gi // 8))
        i = GP.dma_start(out=blk["dst"], in_=blk["src"])
        i.then_inc(sWp[gi % 8], 16)

    ctx.close()
    _BUILD_CACHE["nc"] = nc
    return nc


def _prep_shared(inputs):
    Wq = np.ascontiguousarray(inputs["L0_attn_Wq"], dtype=np.float32)
    Wk = np.ascontiguousarray(inputs["L0_attn_Wk"], dtype=np.float32)
    Wv = np.ascontiguousarray(inputs["L0_attn_Wv"], dtype=np.float32)
    Wo = np.ascontiguousarray(inputs["L0_attn_Wo"], dtype=np.float32)
    Rm = np.ascontiguousarray(inputs["L0_attn_rand_matrix"], dtype=np.float32)
    wg = np.ascontiguousarray(inputs["L0_ffn_wq_gate"], dtype=np.int32)
    wu = np.ascontiguousarray(inputs["L0_ffn_wq_up"], dtype=np.int32)
    wd = np.ascontiguousarray(inputs["L0_ffn_wq_down"], dtype=np.int32)

    def pair_blocks(W, npair):
        blocks = np.empty((2 * npair, 128, 2048), dtype=W.dtype)
        for P in range(npair):
            for half in range(2):
                cols = np.concatenate([
                    np.arange((2 * P) * 128 + half * 64, (2 * P) * 128 + half * 64 + 64),
                    np.arange((2 * P + 1) * 128 + half * 64, (2 * P + 1) * 128 + half * 64 + 64)])
                sel = W[cols, :]
                t = sel.T.reshape(16, 128, 128)
                blocks[2 * P + half] = t.transpose(1, 0, 2).reshape(128, 2048)
        return np.ascontiguousarray(blocks)

    def plane_blocks(W, nplane):
        blocks = np.empty((nplane, 128, 2048), dtype=W.dtype)
        for j in range(nplane):
            sel = W[j * 128:(j + 1) * 128, :]
            t = sel.T.reshape(16, 128, 128)
            blocks[j] = t.transpose(1, 0, 2).reshape(128, 2048)
        return np.ascontiguousarray(blocks)

    def _u8(blocks):
        return np.ascontiguousarray(blocks.view(np.uint8)[:, :, 0::4])

    def down_blocks(W):
        blocks = np.empty((NHC, 128, 2048), dtype=W.dtype)
        for hc in range(NHC):
            sel = W[:, hc * 128:(hc + 1) * 128]
            t = sel.T
            blocks[hc] = t
        return np.ascontiguousarray(blocks)

    rts = np.empty((128, 128), dtype=np.float32)
    p = np.arange(128)
    rts[:, 0:64] = Rm[:, p % 64].T
    rts[:, 64:128] = Rm[:, 64 + (p % 64)].T

    bias = np.zeros((128, 160), dtype=np.float32)
    bq = np.asarray(inputs["L0_attn_bq"], dtype=np.float32)
    bk = np.asarray(inputs["L0_attn_bk"], dtype=np.float32)
    bv = np.asarray(inputs["L0_attn_bv"], dtype=np.float32)
    bo = np.asarray(inputs["L0_attn_bo"], dtype=np.float32)
    for P in range(8):
        for half in range(2):
            col = np.concatenate([bq[(2 * P) * 128 + half * 64:(2 * P) * 128 + half * 64 + 64],
                                  bq[(2 * P + 1) * 128 + half * 64:(2 * P + 1) * 128 + half * 64 + 64]])
            bias[:, BC_BQ + 2 * P + half] = col
    for P in range(4):
        for half in range(2):
            col = np.concatenate([bk[(2 * P) * 128 + half * 64:(2 * P) * 128 + half * 64 + 64],
                                  bk[(2 * P + 1) * 128 + half * 64:(2 * P + 1) * 128 + half * 64 + 64]])
            bias[:, BC_BK + 2 * P + half] = col
    for j in range(8):
        bias[:, BC_BV + j] = bv[j * 128:(j + 1) * 128]
    for cc in range(16):
        bias[:, BC_BO + cc] = bo[cc * 128:(cc + 1) * 128]
    sg = np.asarray(inputs["L0_ffn_s_gate"], dtype=np.float32)
    su = np.asarray(inputs["L0_ffn_s_up"], dtype=np.float32)
    sd = np.asarray(inputs["L0_ffn_s_down"], dtype=np.float32)
    for hc in range(NHC):
        bias[:, BC_SG + hc] = sg[hc * 128:(hc + 1) * 128]
        bias[:, BC_SU + hc] = su[hc * 128:(hc + 1) * 128]
    for cc in range(16):
        bias[:, BC_SD + cc] = sd[cc * 128:(cc + 1) * 128]
    bias[:, BC_CMAG] = CMAG
    bias[64:128, BC_B025] = 0.25
    bias[64:128, BC_B2] = PI / 2
    bias[:, BC_EPS] = 1e-5
    bias[:, BC_EPS12] = 1e-12

    cbf = np.zeros((128, 520), dtype=ml_dtypes.bfloat16)
    cbf[:, CB_ID:CB_ID + 128] = np.eye(128, dtype=ml_dtypes.bfloat16)
    cbf[0:64, CB_MA:CB_MA + 128] = 1
    cbf[64:128, CB_MB:CB_MB + 128] = 1
    cbf[:, CB_ONEC] = 1
    cbf[:, CB_OAVG:CB_OAVG + 128] = ml_dtypes.bfloat16(1.0 / 2048.0)

    return dict(
        wqs=pair_blocks(Wq, 8), wks=pair_blocks(Wk, 4),
        wvs=plane_blocks(Wv, 8), wos=plane_blocks(Wo, 16),
        wgs=_u8(plane_blocks(wg, NHC)), wus=_u8(plane_blocks(wu, NHC)),
        wds=_u8(down_blocks(wd)),
        rts=rts, biasblob=bias, cbf=cbf,
        ones_r_bf=np.ones((1, 128), dtype=ml_dtypes.bfloat16),
        ones_r_f32=np.ones((1, 128), dtype=np.float32),
    )


def _prep_core(x, c):
    def fmajor(sl):
        t = sl.transpose(2, 0, 1)
        t = t.reshape(16, 128, 2, 512)
        return np.ascontiguousarray(t.transpose(1, 0, 2, 3).reshape(128, 16 * 1024))

    xc = fmajor(x[:, 512 * c:512 * c + 512, :])
    u = np.arange(512)
    sp = 256 * (u // 32) + 32 * c + (u % 32)
    xs = fmajor(x[:, sp, :])
    inv_freq = 1.0 / (10000.0 ** (np.arange(0, 128, 2, dtype=np.float64) / 128))
    pos = np.arange(512 * c, 512 * c + 512, dtype=np.float64)
    ang = pos[None, :] * inv_freq[np.arange(128) % 64][:, None]
    return dict(xc=xc, xs=xs,
                cosd=np.cos(ang).astype(ml_dtypes.bfloat16),
                sind=np.sin(ang).astype(ml_dtypes.bfloat16)), sp


def kernel(**inputs):
    x = np.ascontiguousarray(np.asarray(inputs["L0_attn_hidden_states"], dtype=np.float32))
    shared = _prep_shared(inputs)
    nc = _build()
    in_maps = []
    sps = []
    for c in range(NCORES):
        per, sp = _prep_core(x, c)
        sps.append(sp)
        m = dict(shared)
        m.update(per)
        in_maps.append(m)
    res = run_bass_kernel_spmd(nc, in_maps, core_ids=list(range(NCORES)))
    y = np.empty((2, 4096, 2048), dtype=np.float32)
    for c in range(NCORES):
        yT = res.results[c]["yT"]
        z = yT.reshape(128, 16, 2, 512).transpose(2, 3, 1, 0).reshape(2, 512, 2048)
        y[0, sps[c]] = z[0]
        y[1, sps[c]] = z[1]
    return y
